# version 07
# baseline (speedup 1.0000x reference)
import os
import numpy as np
import ml_dtypes
import jax
try:
    os.makedirs("/tmp/jaxcache", exist_ok=True)
    jax.config.update("jax_compilation_cache_dir", "/tmp/jaxcache")
    jax.config.update("jax_persistent_cache_min_entry_size_bytes", -1)
    jax.config.update("jax_persistent_cache_min_compile_time_secs", 0.5)
except Exception:
    pass
import concourse.bass as bass
import concourse.mybir as mybir
from concourse.bass_utils import run_bass_kernel_spmd

F32 = mybir.dt.float32
BF16 = mybir.dt.bfloat16
F16 = mybir.dt.float16
I32 = mybir.dt.int32
U16 = mybir.dt.uint16
U8 = mybir.dt.uint8
AF = mybir.ActivationFunctionType
ALU = mybir.AluOpType
AX = mybir.AxisListType

B, S, E = 2, 4096, 2048
NCH = E // 128
DH_A = 128
I_F = 5632
EPS_LN = 1e-5
EPS_RMS = 1e-6
EPS_GLA = 1e-5
NCORE = 8
RSD = float(1.0 / np.sqrt(128.0))


class Ctx:
    def __init__(self, nc):
        self.nc = nc
        self.t = nc.tensor
        self.v = nc.vector
        self.s = nc.scalar
        self.g = nc.gpsimd
        self.y = nc.sync
        self.sems = {}
        self.cnt = {}
        for n in ("ld", "gst", "mm", "ev", "sc", "st"):
            self.sems[n] = nc.alloc_semaphore(n)
            self.cnt[n] = 0

    def inc(self, name, by):
        self.cnt[name] += by
        return self.sems[name], self.cnt[name]

    def at(self, name):
        return self.sems[name], self.cnt[name]

    def done(self, ins, name, by=1):
        self.cnt[name] += by
        ins.then_inc(self.sems[name], by)

    def w(self, eng, *names):
        for n in names:
            eng.wait_ge(*self.at(n))


def sbuf(nc, name, shape, dtype):
    return nc.alloc_sbuf_tensor(name, shape, dtype).ap()


def psum(nc, name, shape, dtype=F32):
    return nc.alloc_psum_tensor(name, shape, dtype).ap()


def hilo(c, hi, lo, src):
    v = c.v
    c.done(v.tensor_copy(hi, src), "ev")
    c.w(v, "ev")
    return v.tensor_tensor(lo, src, hi, ALU.subtract)


def mm3(t, ps, whi, wlo, xhi, xlo, start, stop):
    t.matmul(ps, whi, xhi, start=start, stop=False)
    t.matmul(ps, whi, xlo, start=False, stop=False)
    return t.matmul(ps, wlo, xhi, start=False, stop=stop)


def load_split_w(c, dram, dst_hi, dst_lo, stage, n_ch, col_sl=None):
    y, v = c.y, c.v
    for ic in range(n_ch):
        src = dram[ic * 128:(ic + 1) * 128, :] if col_sl is None else \
            dram[ic * 128:(ic + 1) * 128, col_sl]
        c.w(y, "ev")
        c.done(y.dma_start(out=stage[:, ic % 2, :], in_=src), "ld", 16)
        c.w(v, "ld")
        c.done(hilo(c, dst_hi[:, ic, :], dst_lo[:, ic, :], stage[:, ic % 2, :]), "ev")


def ln_stats(c, ps1, ps2, ones, xsrc_chunks, sq, strow, scr_h, tsl, eps, rms_only=False):
    t, v, s, g = c.t, c.v, c.s, c.g
    n = len(xsrc_chunks)
    NT = sq.shape[1]
    for ic, xc in enumerate(xsrc_chunks):
        c.w(v, "mm")
        v.tensor_tensor(sq[:], xc, xc, ALU.mult)
        c.done(v.engine_nop(), "ev")
        c.w(t, "ev")
        if not rms_only:
            t.matmul(ps1[0:1, 0:NT], ones[:], xc, start=(ic == 0), stop=(ic == n - 1))
        ins = t.matmul(ps2[0:1, 0:NT], ones[:], sq[:], start=(ic == 0),
                       stop=(ic == n - 1))
        c.done(ins, "mm")
    c.w(v, "mm")
    d = 1.0 / (128 * n)
    if rms_only:
        v.tensor_scalar(strow[0:1, 1, :], ps2[0:1, 0:NT], d, eps, ALU.mult, ALU.add)
    else:
        v.tensor_scalar(strow[0:1, 0, :], ps1[0:1, 0:NT], d, None, ALU.mult)
        v.tensor_scalar(strow[0:1, 1, :], ps2[0:1, 0:NT], d, None, ALU.mult)
        v.tensor_tensor(strow[0:1, 2, :], strow[0:1, 0, :], strow[0:1, 0, :], ALU.mult)
        v.tensor_tensor(strow[0:1, 1, :], strow[0:1, 1, :], strow[0:1, 2, :],
                        ALU.subtract)
        v.tensor_scalar(strow[0:1, 1, :], strow[0:1, 1, :], 1.0, eps, ALU.mult, ALU.add)
    v.reciprocal(strow[0:1, 2, :], strow[0:1, 1, :])
    c.done(v.engine_nop(), "ev")
    c.w(s, "ev")
    c.done(s.activation(strow[0:1, 1, :], strow[0:1, 2, :], AF.Sqrt), "sc")
    c.w(g, "sc")
    if not rms_only:
        c.done(g.dma_start(out=scr_h[0:1, tsl], in_=strow[0:1, 0, :]), "gst", 16)
    c.done(g.dma_start(out=scr_h[1:2, tsl], in_=strow[0:1, 1, :]), "gst", 16)


def build_l1(S=S, E=E, NH=4):
    NCH = E // 128
    nc = bass.Bass("TRN2", target_bir_lowering=False, detect_race_conditions=False)
    c = Ctx(nc)
    xb_h = nc.dram_tensor("xb", [E, S], F32, kind="ExternalInput").ap()
    w_h = {n: nc.dram_tensor(n, [E, NH * 128], F32, kind="ExternalInput").ap()
           for n in ("wq_t", "wk_t", "wv_t")}
    aout_h = nc.dram_tensor("aout", [S, NH * 128], F32, kind="ExternalOutput").ap()
    scr_h = nc.dram_tensor("scr", [4, S], F32, kind="Internal").ap()
    xn_h = nc.dram_tensor("xn_scr", [2, E, S], BF16, kind="Internal").ap()

    NT = 512
    NTT = S // NT
    NKT = S // 128
    t, v, s, g, y = c.t, c.v, c.s, c.g, c.y

    xt = sbuf(nc, "xt", [128, 2, NT], F32)
    sq = sbuf(nc, "sq", [128, NT], F32)
    xnc = sbuf(nc, "xnc", [128, NT], F32)
    ones = sbuf(nc, "ones", [128, 1], F32)
    strow = sbuf(nc, "strow", [1, 4, NT], F32)
    mu_b = sbuf(nc, "mu_b", [128, NT], F32)
    rs_b = sbuf(nc, "rs_b", [128, NT], F32)
    xh = sbuf(nc, "xh", [128, NCH, NT], BF16)
    xl = sbuf(nc, "xl", [128, NCH, NT], BF16)
    wst = sbuf(nc, "wst", [128, 2, 128], F32)
    wts = {wn: (sbuf(nc, wn + "h", [128, NCH, 128], BF16),
                sbuf(nc, wn + "l", [128, NCH, 128], BF16))
           for wn in ("wq", "wk", "wv")}
    qhh = sbuf(nc, "qhh", [128, S], BF16)
    qhl = sbuf(nc, "qhl", [128, S], BF16)
    khh = sbuf(nc, "khh", [128, S], BF16)
    khl = sbuf(nc, "khl", [128, S], BF16)
    vtm = sbuf(nc, "vtm", [128, NKT, 129], F32)
    pt = sbuf(nc, "pt", [128, 512], F32)
    att = sbuf(nc, "att", [128, 128], F32)
    rl = sbuf(nc, "rl", [128, 1], F32)
    NQT = S // 128
    m_all = sbuf(nc, "m_all", [128, NKT], F32)
    ecol = sbuf(nc, "ecol", [128, NQT * NKT], F32)
    mr = sbuf(nc, "mr", [128, 1], F32)
    mb16 = sbuf(nc, "mb16", [128, 1], BF16)
    mbf = sbuf(nc, "mbf", [128, 1], F32)
    negmb = sbuf(nc, "negmb", [128, 1], F32)
    mrow = sbuf(nc, "mrow", [1, 512], F32)
    ones128 = sbuf(nc, "ones128", [128, 128], BF16)
    mpad = sbuf(nc, "mpad", [128, 512], BF16)
    acc = sbuf(nc, "acc", [128, 4, 128], F32)
    lacc = sbuf(nc, "lacc", [128, 4], F32)
    wz = sbuf(nc, "wz", [128, 4], F32)

    psA = psum(nc, "psA", [128, 512])
    psB = psum(nc, "psB", [128, 512])
    pv = [psum(nc, f"pv{i}", [128, 129]) for i in range(4)]

    with nc.Block() as block:
        @block.sync
        def _(sync):
            v.memset(ones[:], 1.0)
            v.memset(ones128[:], 0.0)
            v.memset(mpad[:], 0.0)
            c.done(v.memset(ones128[0:1, :], 1.0), "ev")

            for tt in range(NTT):
                tsl = slice(tt * NT, (tt + 1) * NT)
                xs = []
                for ic in range(NCH):
                    c.w(y, "ev", "mm")
                    c.done(y.dma_start(out=xt[:, ic % 2, :],
                                       in_=xb_h[ic * 128:(ic + 1) * 128, tsl]),
                           "ld", 16)
                    c.w(v, "ld")
                    c.w(v, "mm")
                    c.done(v.tensor_tensor(sq[:], xt[:, ic % 2, :], xt[:, ic % 2, :],
                                           ALU.mult), "ev")
                    c.w(t, "ev")
                    t.matmul(psA[0:1, 0:NT], ones[:], xt[:, ic % 2, :],
                             start=(ic == 0), stop=(ic == NCH - 1))
                    ins = t.matmul(psB[0:1, 0:NT], ones[:], sq[:],
                                   start=(ic == 0), stop=(ic == NCH - 1))
                    c.done(ins, "mm")
                c.w(v, "mm")
                for ins_ in (
                    lambda: v.tensor_scalar(strow[0:1, 0, :], psA[0:1, 0:NT], 1.0 / E, None, ALU.mult),
                    lambda: v.tensor_scalar(strow[0:1, 1, :], psB[0:1, 0:NT], 1.0 / E, None, ALU.mult),
                    lambda: v.tensor_tensor(strow[0:1, 2, :], strow[0:1, 0, :], strow[0:1, 0, :], ALU.mult),
                    lambda: v.tensor_tensor(strow[0:1, 1, :], strow[0:1, 1, :], strow[0:1, 2, :], ALU.subtract),
                    lambda: v.tensor_scalar(strow[0:1, 1, :], strow[0:1, 1, :], 1.0, EPS_LN, ALU.mult, ALU.add),
                    lambda: v.reciprocal(strow[0:1, 2, :], strow[0:1, 1, :]),
                ):
                    c.done(ins_(), "ev")
                    c.w(v, "ev")
                c.w(s, "ev")
                c.done(s.activation(strow[0:1, 1, :], strow[0:1, 2, :], AF.Sqrt), "sc")
                c.w(g, "sc")
                c.done(g.dma_start(out=scr_h[0:1, tsl], in_=strow[0:1, 0, :]), "gst", 16)
                c.done(g.dma_start(out=scr_h[1:2, tsl], in_=strow[0:1, 1, :]), "gst", 16)
                c.w(y, "gst")
                c.done(y.dma_start(out=mu_b[:], in_=scr_h[0:1, tsl].broadcast_to([128, NT])), "ld", 16)
                c.done(y.dma_start(out=rs_b[:], in_=scr_h[1:2, tsl].broadcast_to([128, NT])), "ld", 16)
                for ic in range(NCH):
                    c.w(y, "gst")
                    c.done(y.dma_start(out=xt[:, ic % 2, :],
                                       in_=xb_h[ic * 128:(ic + 1) * 128, tsl]), "ld", 16)
                    c.w(v, "ld")
                    v.tensor_tensor(xnc[:], xt[:, ic % 2, :], mu_b[:], ALU.subtract)
                    v.tensor_tensor(xnc[:], xnc[:], rs_b[:], ALU.mult)
                    c.done(hilo(c, xh[:, ic, :], xl[:, ic, :], xnc[:]), "ev")
                    c.w(g, "ev")
                    c.done(g.dma_start(out=xn_h[0, ic * 128:(ic + 1) * 128, tsl],
                                       in_=xh[:, ic, :]), "gst", 16)
                    c.done(g.dma_start(out=xn_h[1, ic * 128:(ic + 1) * 128, tsl],
                                       in_=xl[:, ic, :]), "gst", 16)

            for head in range(NH):
                hsl = slice(head * 128, (head + 1) * 128)
                for wn in ("wq", "wk", "wv"):
                    load_split_w(c, w_h[wn + "_t"], wts[wn][0], wts[wn][1],
                                 wst, NCH, col_sl=hsl)

                for tt in range(NTT):
                    tsl = slice(tt * NT, (tt + 1) * NT)
                    c.w(y, "mm")
                    for ic in range(NCH):
                        c.done(y.dma_start(out=xh[:, ic, :],
                                           in_=xn_h[0, ic * 128:(ic + 1) * 128, tsl]), "ld", 16)
                        c.done(y.dma_start(out=xl[:, ic, :],
                                           in_=xn_h[1, ic * 128:(ic + 1) * 128, tsl]), "ld", 16)
                    c.w(t, "ld", "ev")
                    for wn, dh, dl in (("wq", qhh, qhl), ("wk", khh, khl)):
                        whi, wlo = wts[wn]
                        for ic in range(NCH):
                            ins = mm3(t, psA[:, 0:NT], whi[:, ic, :], wlo[:, ic, :],
                                      xh[:, ic, :], xl[:, ic, :],
                                      start=(ic == 0), stop=(ic == NCH - 1))
                        c.done(ins, "mm")
                        c.w(v, "mm")
                        c.done(hilo(c, dh[:, tsl], dl[:, tsl], psA[:, 0:NT]), "ev")
                        c.w(t, "ev")
                    whi, wlo = wts["wv"]
                    for sub in range(NT // 128):
                        kt = tt * (NT // 128) + sub
                        ssl = slice(tt * NT + sub * 128, tt * NT + sub * 128 + 128)
                        ssl_l = slice(sub * 128, (sub + 1) * 128)
                        for ic in range(NCH):
                            ins = mm3(t, psB[:, 0:128],
                                      xh[:, ic, ssl_l], xl[:, ic, ssl_l],
                                      whi[:, ic, :], wlo[:, ic, :],
                                      start=(ic == 0), stop=(ic == NCH - 1))
                        c.done(ins, "mm")
                        c.w(v, "mm")
                        v.tensor_copy(vtm[:, kt, 0:128], psB[:, 0:128])
                        c.done(v.memset(vtm[:, kt, 128:129], 1.0), "ev")
                        c.w(t, "ev")

                sem_mm, sem_ev = c.sems["mm"], c.sems["ev"]
                NKC = S // 512
                for qt in range(NQT):
                    qtsl = slice(qt * 128, (qt + 1) * 128)
                    mm_sA = {}
                    ev_rA = {}

                    def emit_scoresA(kc):
                        ksl = slice(kc * 512, (kc + 1) * 512)
                        stb = psA if kc % 2 == 0 else psB
                        if kc < 2:
                            c.w(t, "ev", "sc")
                        else:
                            t.wait_ge(sem_ev, ev_rA[kc - 2])
                        t.matmul(stb[:], qhh[:, qtsl], khh[:, ksl], start=True, stop=False)
                        t.matmul(stb[:], qhh[:, qtsl], khl[:, ksl], start=False, stop=False)
                        ins = t.matmul(stb[:], qhl[:, qtsl], khh[:, ksl],
                                       start=False, stop=True)
                        c.done(ins, "mm")
                        mm_sA[kc] = c.cnt["mm"]

                    emit_scoresA(0)
                    for kc in range(NKC):
                        if kc + 1 < NKC:
                            emit_scoresA(kc + 1)
                        stb = psA if kc % 2 == 0 else psB
                        v.wait_ge(sem_mm, mm_sA[kc])
                        if kc == 0:
                            c.w(v, "sc", "gst")
                        for j in range(4):
                            ins = v.tensor_reduce(m_all[:, kc * 4 + j:kc * 4 + j + 1],
                                                  stb[:, j * 128:(j + 1) * 128],
                                                  AX.X, ALU.max)
                        c.done(ins, "ev")
                        ev_rA[kc] = c.cnt["ev"]
                    c.w(v, "ev")
                    for ins_ in (
                        lambda: v.tensor_reduce(mr[:], m_all[:, 0:NKT], AX.X, ALU.max),
                        lambda: v.tensor_copy(mb16[:], mr[:]),
                        lambda: v.tensor_copy(mbf[:], mb16[:]),
                        lambda: v.tensor_scalar(negmb[:], mbf[:], -1.0, None, ALU.mult),
                        lambda: v.tensor_scalar(m_all[:, 0:NKT], m_all[:, 0:NKT],
                                                negmb[:, 0:1], None, ALU.add),
                        lambda: v.tensor_scalar(m_all[:, 0:NKT], m_all[:, 0:NKT],
                                                -984.0, None, ALU.max),
                    ):
                        c.done(ins_(), "ev")
                        c.w(v, "ev")
                    c.w(s, "ev")
                    c.done(s.activation(ecol[:, qt * NKT:qt * NKT + NKT],
                                        m_all[:, 0:NKT], AF.Exp, scale=-RSD), "sc")
                    c.w(g, "ev")
                    c.done(g.dma_start(out=scr_h[2:3, qtsl], in_=negmb[:, 0:1]),
                           "gst", 16)
                for qg in range(S // 512):
                    qsl = slice(qg * 512, (qg + 1) * 512)
                    c.w(y, "gst", "ev")
                    c.done(y.dma_start(out=mrow[0:1, :], in_=scr_h[2:3, qsl]), "ld", 16)
                    c.w(v, "ld", "sc")
                    c.done(v.tensor_copy(mpad[0:1, :], mrow[0:1, :]), "ev")
                    v.memset(acc[:], 0.0)
                    c.done(v.memset(lacc[:], 0.0), "ev")
                    for kt in range(NKT):
                        ksl = slice(kt * 128, (kt + 1) * 128)
                        stb = psA if kt % 2 == 0 else psB
                        c.w(t, "ev", "sc")
                        t.matmul(stb[:], khh[:, ksl], qhh[:, qsl], start=True, stop=False)
                        t.matmul(stb[:], khh[:, ksl], qhl[:, qsl], start=False, stop=False)
                        t.matmul(stb[:], khl[:, ksl], qhh[:, qsl], start=False, stop=False)
                        ins = t.matmul(stb[:], ones128[:], mpad[:],
                                       start=False, stop=True)
                        c.done(ins, "mm")
                        c.w(s, "mm")
                        c.done(s.activation(pt[:, 0:512], stb[:], AF.Exp,
                                            scale=RSD), "sc")
                        c.w(t, "sc", "ev")
                        for qs in range(4):
                            ins = t.matmul(pv[qs][:], pt[:, qs * 128:(qs + 1) * 128],
                                           vtm[:, kt, :], start=True, stop=True)
                        c.done(ins, "mm")
                        c.w(v, "mm")
                        for qs in range(4):
                            qtg = qg * 4 + qs
                            ins = v.tensor_tensor(wz[:, qs:qs + 1], pv[qs][:, 128:129],
                                                  ecol[:, qtg * NKT + kt:qtg * NKT + kt + 1],
                                                  ALU.mult)
                        c.done(ins, "ev")
                        c.w(v, "ev")
                        for qs in range(4):
                            v.tensor_tensor(lacc[:, qs:qs + 1], lacc[:, qs:qs + 1],
                                            pv[qs][:, 128:129], ALU.add)
                            ins = v.scalar_tensor_tensor(acc[:, qs, :], pv[qs][:, 0:128],
                                                         wz[:, qs:qs + 1], acc[:, qs, :],
                                                         ALU.mult, ALU.add)
                        c.done(ins, "ev")
                        c.w(v, "ev")
                    for qs in range(4):
                        c.w(v, "sc")
                        c.done(v.reciprocal(rl[:], lacc[:, qs:qs + 1]), "ev")
                        c.w(s, "ev", "st")
                        c.done(s.activation(att[:], acc[:, qs, :], AF.Copy,
                                            scale=rl[:]), "sc")
                        c.w(g, "sc")
                        tok0 = qg * 512 + qs * 128
                        c.done(g.dma_start(out=aout_h[tok0:tok0 + 128, hsl],
                                           in_=att[:]), "st", 16)
                c.w(t, "st")

            c.w(g, "st")
    return nc





def emit_dequant_tile(c, bufs, packed_sl, sexp_sl, W):
    v, y = c.v, c.y
    byts, nib, m_, tmp, sexp, wf, whi, wlo = bufs
    WB = W // 2
    c.w(y, "ev")
    c.done(y.dma_start(out=byts[:, 0:W // 4].bitcast(I32), in_=packed_sl), "ld", 16)
    c.done(y.dma_start(out=sexp[:, 0:W], in_=sexp_sl), "ld", 16)
    c.w(v, "ld")
    v.tensor_copy(m_[:, 0:WB], byts[:, 0:W // 4].bitcast(U8))
    v.tensor_single_scalar(nib[:, 0:WB], m_[:, 0:WB], 15, ALU.bitwise_and)
    v.tensor_single_scalar(nib[:, WB:W], m_[:, 0:WB], 4, ALU.logical_shift_right)
    v.tensor_single_scalar(m_[:, 0:W], nib[:, 0:W], 7, ALU.bitwise_and)
    v.tensor_tensor(tmp[:, 0:W], nib[:, 0:W], m_[:, 0:W], ALU.subtract)
    v.tensor_scalar(tmp[:, 0:W], tmp[:, 0:W], 4096, None, ALU.mult)
    v.scalar_tensor_tensor(tmp[:, 0:W], m_[:, 0:W], 64, tmp[:, 0:W], ALU.mult, ALU.add)
    v.tensor_single_scalar(nib[:, 0:W], m_[:, 0:W], 1, ALU.min)
    v.scalar_tensor_tensor(tmp[:, 0:W], nib[:, 0:W], 0x3E80, tmp[:, 0:W], ALU.mult, ALU.add)
    v.tensor_single_scalar(nib[:, 0:W], m_[:, 0:W], 2, ALU.min)
    v.scalar_tensor_tensor(tmp[:, 0:W], nib[:, 0:W], 64, tmp[:, 0:W], ALU.mult, ALU.add)
    v.tensor_copy(wf[:, 0:W], sexp[:, 0:W])
    v.tensor_tensor(wf[:, 0:W], wf[:, 0:W], tmp[:, 0:W].bitcast(BF16), ALU.mult)
    v.tensor_copy(whi[:, 0:W], wf[:, 0:W])
    c.done(v.tensor_tensor(wlo[:, 0:W], wf[:, 0:W], whi[:, 0:W], ALU.subtract), "ev")


def emit_transpose_store(c, whi, wlo, wtt, dst, oc, span0, W):
    y, g = c.y, c.g
    nblk = W // 128
    c.w(y, "ev")
    for hl, wsrc in ((0, whi), (1, wlo)):
        for k in range(nblk):
            half = 0 if k < nblk // 2 else 1
            kk = k if k < nblk // 2 else k - nblk // 2
            c.w(y, "gst")
            c.done(y.dma_start_transpose(wtt[:, (k + hl * nblk) % 2, :],
                                         wsrc[:, k * 128:(k + 1) * 128]), "ld", 16)
            c.w(g, "ld")
            r0 = span0 + 256 * kk + half
            c.done(g.dma_start(
                out=dst[hl, r0:r0 + 255:2, oc * 128:(oc + 1) * 128],
                in_=wtt[:, (k + hl * nblk) % 2, :]), "gst", 16)


def stream_w_block(c, dram, oc, n_ch, wsp, wbh, wbl, width=128):
    y, v = c.y, c.v
    csl = slice(oc * width, (oc + 1) * width)
    c.w(y, "mm")
    c.done(y.dma_start(
        out=wsp[:, 0:n_ch, 0:width],
        in_=dram[:, csl].rearrange("(ic p) o -> p ic o", p=128)), "ld", 16)
    c.w(v, "ld")
    for ic in range(n_ch):
        ins = hilo(c, wbh[:, ic, 0:width], wbl[:, ic, 0:width], wsp[:, ic, 0:width])
    c.done(ins, "ev")


def ln_apply_split(c, xsrc, mu_b, rs_b, xnc, dsth, dstl):
    v = c.v
    v.tensor_tensor(xnc, xsrc, mu_b, ALU.subtract)
    v.tensor_tensor(xnc, xnc, rs_b, ALU.mult)
    return hilo(c, dsth, dstl, xnc)


def build_l2(NTok=1024, E_=E, IF_=I_F, mode=2):
    NCH_ = E_ // 128
    NHC = IF_ // 128
    NT = 512
    NH2 = NTok // NT
    TF = 256
    nc = bass.Bass("TRN2", target_bir_lowering=False, detect_race_conditions=False)
    c = Ctx(nc)
    t, v, s, g, y = c.t, c.v, c.s, c.g, c.y

    if mode == 2:
        aout_h = nc.dram_tensor("aout_s", [E_, NTok], F32, kind="ExternalInput").ap()
        x_h = nc.dram_tensor("x_s", [E_, NTok], F32, kind="ExternalInput").ap()
        wout_h = nc.dram_tensor("wout_t", [E_, E_], F32, kind="ExternalInput").ap()
        bout_h = nc.dram_tensor("bout", [128, NCH_], F32, kind="ExternalInput").ap()
    else:
        o_in_h = nc.dram_tensor("o_s", [E_, NTok], F32, kind="ExternalInput").ap()
        gate_h = nc.dram_tensor("g_s", [E_, NTok], F32, kind="ExternalInput").ap()
        x_h = nc.dram_tensor("x_s", [E_, NTok], F32, kind="ExternalInput").ap()
        wout_h = nc.dram_tensor("wo_t", [E_, E_], F32, kind="ExternalInput").ap()
    gup_h = nc.dram_tensor("gup_p", [2 * IF_, E_ // 8], I32, kind="ExternalInput").ap()
    gus_h = nc.dram_tensor("gup_se", [2 * IF_, E_], F16, kind="ExternalInput").ap()
    dnp_h = nc.dram_tensor("dn_p", [E_, IF_ // 8], I32, kind="ExternalInput").ap()
    dns_h = nc.dram_tensor("dn_se", [E_, IF_], F16, kind="ExternalInput").ap()
    if mode == 2:
        wq_h = nc.dram_tensor("wqg_t", [E_, E_ // 2], F32, kind="ExternalInput").ap()
        wk_h = nc.dram_tensor("wkg_t", [E_, E_ // 2], F32, kind="ExternalInput").ap()
        wkg1_h = nc.dram_tensor("wkg1_t", [E_, 16], F32, kind="ExternalInput").ap()
        wkg2_h = nc.dram_tensor("wkg2_t", [16, E_ // 2], F32, kind="ExternalInput").ap()
        bkg2_h = nc.dram_tensor("bkg2", [128, E_ // 256], F32, kind="ExternalInput").ap()
        wv_h = nc.dram_tensor("wvg_t", [E_, E_], F32, kind="ExternalInput").ap()
        wg_h = nc.dram_tensor("wg_t", [E_, E_], F32, kind="ExternalInput").ap()
        bg_h = nc.dram_tensor("bg", [128, NCH_], F32, kind="ExternalInput").ap()
        x2_o = nc.dram_tensor("x2_o", [E_, NTok], F32, kind="ExternalOutput").ap()
        g_o = nc.dram_tensor("g_o", [E_, NTok], F32, kind="ExternalOutput").ap()
        q_o = nc.dram_tensor("q_o", [E_ // 2, NTok], F32, kind="ExternalOutput").ap()
        k_o = nc.dram_tensor("k_o", [E_ // 2, NTok], F32, kind="ExternalOutput").ap()
        gk_o = nc.dram_tensor("gk_o", [E_ // 2, NTok], F32, kind="ExternalOutput").ap()
        v_o = nc.dram_tensor("v_o", [NTok, E_], F32, kind="ExternalOutput").ap()
    else:
        x2_o = nc.dram_tensor("xout_o", [E_, NTok], F32, kind="ExternalOutput").ap()

    wgu_s = nc.dram_tensor("wgu_s", [2, E_, 2 * IF_], BF16, kind="Internal").ap()
    wdn_s = nc.dram_tensor("wdn_s", [2, IF_, E_], BF16, kind="Internal").ap()
    h_s = nc.dram_tensor("h_s", [2, IF_, NT], BF16, kind="Internal").ap()
    scr_h = nc.dram_tensor("scr", [4, 1024], F32, kind="Internal").ap()

    x1 = sbuf(nc, "x1", [128, NCH_, NT], F32)
    xnh = sbuf(nc, "xnh", [128, NCH_, NT], BF16)
    xnl = sbuf(nc, "xnl", [128, NCH_, NT], BF16)
    wsp = sbuf(nc, "wsp", [128, NCH_, 128], F32)
    wbh = sbuf(nc, "wbh", [128, NCH_, 128], BF16)
    wbl = sbuf(nc, "wbl", [128, NCH_, 128], BF16)
    ast = sbuf(nc, "ast", [128, NT], F32)
    sq = sbuf(nc, "sq", [128, NT], F32)
    xnc = sbuf(nc, "xnc", [128, NT], F32)
    ones = sbuf(nc, "ones", [128, 1], F32)
    strow = sbuf(nc, "strow", [1, 4, NT], F32)
    mu_b = sbuf(nc, "mu_b", [128, NT], F32)
    rs_b = sbuf(nc, "rs_b", [128, NT], F32)
    bcol = sbuf(nc, "bcol", [128, NCH_], F32)
    bkgcol = sbuf(nc, "bkgcol", [128, max(1, E_ // 256)], F32)
    bgcol = sbuf(nc, "bgcol", [128, NCH_], F32)
    outc = sbuf(nc, "outc", [128, NT], F32)
    DQW = min(2048, max(E_, IF_))
    byts = sbuf(nc, "byts", [128, DQW // 2], U16)
    nib = sbuf(nc, "nib", [128, DQW], U16)
    m_ = sbuf(nc, "m_", [128, DQW], U16)
    tmp = sbuf(nc, "tmp", [128, DQW], U16)
    sexp = sbuf(nc, "sexp", [128, DQW], F16)
    wf = sbuf(nc, "wf", [128, DQW], F32)
    whi = sbuf(nc, "whi", [128, DQW], BF16)
    wlo = sbuf(nc, "wlo", [128, DQW], BF16)
    wtt = sbuf(nc, "wtt", [128, 2, 128], BF16)
    dq = (byts, nib, m_, tmp, sexp, wf, whi, wlo)
    wgu_t = wf[:, 0:DQW].bitcast(BF16)
    hsc = sbuf(nc, "hsc", [128, NT], F32)
    hfc = sbuf(nc, "hfc", [128, NT], F32)
    hh_t = sbuf(nc, "hh_t", [128, 2, NT], BF16)
    yb = sbuf(nc, "yb", [128, NCH_, NT], F32)
    t1h = sbuf(nc, "t1h", [128, NTok], BF16)
    t1l = sbuf(nc, "t1l", [128, NTok], BF16)
    wkg2s = sbuf(nc, "wkg2s", [16, E_ // 2], F32)
    wkg2h = sbuf(nc, "wkg2h", [128, E_ // 2], BF16)
    wkg2l = sbuf(nc, "wkg2l", [128, E_ // 2], BF16)

    ps = psum(nc, "ps", [128, 8, 512])

    NSP = (IF_ + DQW - 1) // DQW
    spans = [min(DQW, IF_ - i * DQW) for i in range(NSP)]

    with nc.Block() as block:
        @block.sync
        def _(sync):
            v.memset(ones[:], 1.0)
            v.memset(t1h[:], 0.0)
            v.memset(t1l[:], 0.0)
            v.memset(wkg2h[:], 0.0)
            c.done(v.memset(wkg2l[:], 0.0), "ev")

            for oc in range(2 * IF_ // 128):
                emit_dequant_tile(c, dq, gup_h[oc * 128:(oc + 1) * 128, :],
                                  gus_h[oc * 128:(oc + 1) * 128, :], E_)
                emit_transpose_store(c, whi, wlo, wtt, wgu_s, oc, 0, E_)
            for oc in range(E_ // 128):
                for sp in range(NSP):
                    W = spans[sp]
                    emit_dequant_tile(
                        c, dq, dnp_h[oc * 128:(oc + 1) * 128,
                                     sp * DQW // 8:(sp * DQW + W) // 8],
                        dns_h[oc * 128:(oc + 1) * 128, sp * DQW:sp * DQW + W], W)
                    emit_transpose_store(c, whi, wlo, wtt, wdn_s, oc, sp * DQW, W)

            if mode == 2:
                c.done(y.dma_start(out=bcol[:, :], in_=bout_h), "ld", 16)

            for hf in range(NH2):
                hsl = slice(hf * NT, (hf + 1) * NT)
                for ic in range(NCH_):
                    c.w(y, "ev", "mm", "sc", "st")
                    c.done(y.dma_start(out=x1[:, ic, :],
                                       in_=x_h[ic * 128:(ic + 1) * 128, hsl]), "ld", 16)
                    if mode == 2:
                        c.done(y.dma_start(out=ast[:],
                                           in_=aout_h[ic * 128:(ic + 1) * 128, hsl]), "ld", 16)
                        c.w(v, "ld")
                        c.done(hilo(c, xnh[:, ic, :], xnl[:, ic, :], ast[:]), "ev")
                    else:
                        c.done(y.dma_start(out=ast[:],
                                           in_=gate_h[ic * 128:(ic + 1) * 128, hsl]), "ld", 16)
                        c.done(y.dma_start(out=xnc[:],
                                           in_=o_in_h[ic * 128:(ic + 1) * 128, hsl]), "ld", 16)
                        c.w(s, "ld")
                        c.done(s.activation(sq[:], ast[:], AF.Silu), "sc")
                        c.w(v, "sc")
                        c.done(v.tensor_tensor(ast[:], sq[:], xnc[:], ALU.mult), "ev")
                        c.w(v, "ev")
                        c.done(hilo(c, xnh[:, ic, :], xnl[:, ic, :], ast[:]), "ev")
                for oc in range(NCH_):
                    stream_w_block(c, wout_h, oc, NCH_, wsp, wbh, wbl)
                    c.w(t, "ld", "ev")
                    for ic in range(NCH_):
                        ins = mm3(t, ps[:, 0, :], wbh[:, ic, :], wbl[:, ic, :],
                                  xnh[:, ic, :], xnl[:, ic, :],
                                  start=(ic == 0), stop=(ic == NCH_ - 1))
                    c.done(ins, "mm")
                    c.w(v, "mm")
                    if mode == 2:
                        v.tensor_scalar(outc[:], ps[:, 0, :], bcol[:, oc:oc + 1], None, ALU.add)
                    else:
                        v.tensor_copy(outc[:], ps[:, 0, :])
                    c.done(v.tensor_tensor(x1[:, oc, :], x1[:, oc, :], outc[:], ALU.add), "ev")
                    c.w(t, "ev")

                ln_stats(c, ps[:, 0, :], ps[:, 1, :], ones,
                         [x1[:, ic, :] for ic in range(NCH_)], sq, strow, scr_h,
                         slice(0, NT), EPS_LN)
                c.w(y, "gst")
                c.done(y.dma_start(out=mu_b[:], in_=scr_h[0:1, 0:NT].broadcast_to([128, NT])), "ld", 16)
                c.done(y.dma_start(out=rs_b[:], in_=scr_h[1:2, 0:NT].broadcast_to([128, NT])), "ld", 16)
                c.w(v, "ld")
                for ic in range(NCH_):
                    c.done(ln_apply_split(c, x1[:, ic, :], mu_b[:], rs_b[:], xnc[:],
                                          xnh[:, ic, :], xnl[:, ic, :]), "ev")

                for g_ in range(NHC):
                    c.w(y, "mm", "gst")
                    for hl in range(2):
                        c.done(y.dma_start(
                            out=wbh[:, :, :] if hl == 0 else wbl[:, :, :],
                            in_=wgu_s[hl, :, g_ * 128:(g_ + 1) * 128]
                            .rearrange("(ic p) o -> p ic o", p=128)), "ld", 16)
                        c.done(y.dma_start(
                            out=wsp[:, :, :].bitcast(BF16)[:, :, 0:128] if hl == 0
                            else wsp[:, :, :].bitcast(BF16)[:, :, 128:256],
                            in_=wgu_s[hl, :, (NHC + g_) * 128:(NHC + g_ + 1) * 128]
                            .rearrange("(ic p) o -> p ic o", p=128)), "ld", 16)
                    wuh = wsp[:, :, :].bitcast(BF16)[:, :, 0:128]
                    wul = wsp[:, :, :].bitcast(BF16)[:, :, 128:256]
                    c.w(t, "ld", "ev")
                    for ic in range(NCH_):
                        t.matmul(ps[:, 0, :], wbh[:, ic, :], xnh[:, ic, :],
                                 start=(ic == 0), stop=False)
                        t.matmul(ps[:, 0, :], wbh[:, ic, :], xnl[:, ic, :], start=False, stop=False)
                        t.matmul(ps[:, 0, :], wbl[:, ic, :], xnh[:, ic, :],
                                 start=False, stop=(ic == NCH_ - 1))
                        t.matmul(ps[:, 1, :], wuh[:, ic, :], xnh[:, ic, :],
                                 start=(ic == 0), stop=False)
                        t.matmul(ps[:, 1, :], wuh[:, ic, :], xnl[:, ic, :], start=False, stop=False)
                        ins = t.matmul(ps[:, 1, :], wul[:, ic, :], xnh[:, ic, :],
                                       start=False, stop=(ic == NCH_ - 1))
                    c.done(ins, "mm")
                    c.w(s, "mm")
                    c.done(s.activation(hsc[:], ps[:, 0, :], AF.Silu), "sc")
                    c.w(v, "sc")
                    v.tensor_tensor(hfc[:], hsc[:], ps[:, 1, :], ALU.mult)
                    c.done(hilo(c, xnc[:].bitcast(BF16)[:, 0:NT],
                                xnc[:].bitcast(BF16)[:, NT:2 * NT], hfc[:]), "ev")
                    c.w(g, "ev")
                    c.done(g.dma_start(out=h_s[0, g_ * 128:(g_ + 1) * 128, :],
                                       in_=xnc[:].bitcast(BF16)[:, 0:NT]), "gst", 16)
                    c.done(g.dma_start(out=h_s[1, g_ * 128:(g_ + 1) * 128, :],
                                       in_=xnc[:].bitcast(BF16)[:, NT:2 * NT]), "gst", 16)
                    c.w(t, "gst")

                for pp in range(2):
                    for hc in range(NHC):
                        c.w(y, "mm", "gst")
                        c.done(y.dma_start(
                            out=wsp[:, 0:NCH_ // 2, :].bitcast(BF16).rearrange("p a b -> p (a b)")[:, 0:E_ // 2],
                            in_=wdn_s[0, hc * 128:(hc + 1) * 128,
                                      pp * (E_ // 2):(pp + 1) * (E_ // 2)]), "ld", 16)
                        c.done(y.dma_start(
                            out=sq[:].bitcast(BF16)[:, 0:E_ // 2],
                            in_=wdn_s[1, hc * 128:(hc + 1) * 128,
                                      pp * (E_ // 2):(pp + 1) * (E_ // 2)]), "ld", 16)
                        c.done(y.dma_start(out=hh_t[:, 0, :],
                                           in_=h_s[0, hc * 128:(hc + 1) * 128, :]), "ld", 16)
                        c.done(y.dma_start(out=hh_t[:, 1, :],
                                           in_=h_s[1, hc * 128:(hc + 1) * 128, :]), "ld", 16)
                        wdh = wsp[:, 0:NCH_ // 2, :].bitcast(BF16).rearrange("p a b -> p (a b)")[:, 0:E_ // 2]
                        wdl = sq[:].bitcast(BF16)[:, 0:E_ // 2]
                        c.w(t, "ld")
                        for o8 in range(NCH_ // 2):
                            bank = ps[:, o8, :]
                            osl = slice(o8 * 128, (o8 + 1) * 128)
                            t.matmul(bank, wdh[:, osl], hh_t[:, 0, :],
                                     start=(hc == 0), stop=False)
                            t.matmul(bank, wdh[:, osl], hh_t[:, 1, :], start=False, stop=False)
                            ins = t.matmul(bank, wdl[:, osl], hh_t[:, 0, :],
                                           start=False, stop=(hc == NHC - 1))
                        c.done(ins, "mm")
                    c.w(v, "mm")
                    for o8 in range(NCH_ // 2):
                        c.done(v.tensor_copy(yb[:, pp * (NCH_ // 2) + o8, :],
                                             ps[:, o8, :]), "ev")
                    c.w(t, "ev")
                ln_stats(c, ps[:, 0, :], ps[:, 1, :], ones,
                         [yb[:, ic2, :] for ic2 in range(NCH_)], sq, strow,
                         scr_h, slice(0, NT), EPS_RMS, rms_only=True)
                c.w(y, "gst")
                c.done(y.dma_start(out=rs_b[:],
                                   in_=scr_h[1:2, 0:NT].broadcast_to([128, NT])), "ld", 16)
                c.w(v, "ld")
                for oc2 in range(NCH_):
                    v.tensor_tensor(outc[:], yb[:, oc2, :], rs_b[:], ALU.mult)
                    c.done(v.tensor_tensor(x1[:, oc2, :], x1[:, oc2, :],
                                           outc[:], ALU.add), "ev")
                c.w(t, "ev")

                c.w(g, "ev")
                for ic in range(NCH_):
                    c.done(g.dma_start(out=x2_o[ic * 128:(ic + 1) * 128, hsl],
                                       in_=x1[:, ic, :]), "st", 16)
                if mode == 4:
                    continue

                ln_stats(c, ps[:, 0, :], ps[:, 1, :], ones,
                         [x1[:, ic, :] for ic in range(NCH_)], sq, strow, scr_h,
                         slice(0, NT), EPS_LN)
                c.w(y, "gst")
                c.done(y.dma_start(out=mu_b[:], in_=scr_h[0:1, 0:NT].broadcast_to([128, NT])), "ld", 16)
                c.done(y.dma_start(out=rs_b[:], in_=scr_h[1:2, 0:NT].broadcast_to([128, NT])), "ld", 16)
                c.w(v, "ld")
                for ic in range(NCH_):
                    c.done(ln_apply_split(c, x1[:, ic, :], mu_b[:], rs_b[:], xnc[:],
                                          xnh[:, ic, :], xnl[:, ic, :]), "ev")

                for wdram, dsto, kscale in ((wq_h, q_o, 1.0), (wk_h, k_o, 1.0 / 16.0)):
                    for oc in range(E_ // 256):
                        stream_w_block(c, wdram, oc, NCH_, wsp, wbh, wbl)
                        c.w(t, "ld", "ev")
                        for ic in range(NCH_):
                            ins = mm3(t, ps[:, 0, :], wbh[:, ic, :], wbl[:, ic, :],
                                      xnh[:, ic, :], xnl[:, ic, :],
                                      start=(ic == 0), stop=(ic == NCH_ - 1))
                        c.done(ins, "mm")
                        c.w(v, "mm")
                        c.done(v.tensor_scalar(outc[:], ps[:, 0, :], kscale, None,
                                               ALU.mult), "ev")
                        c.w(g, "ev")
                        c.done(g.dma_start(out=dsto[oc * 128:(oc + 1) * 128, hsl],
                                           in_=outc[:]), "st", 16)
                        c.w(t, "st")
                wkg1h_g = wbh[:, :, 0:16]
                wkg1l_g = wbl[:, :, 0:16]
                load_split_w(c, wkg1_h, wkg1h_g, wkg1l_g, wsp[:, 0:2, 0:16], NCH_)
                c.w(t, "ld", "ev")
                for ic in range(NCH_):
                    ins = mm3(t, ps[0:16, 0, :], wkg1h_g[:, ic, 0:16], wkg1l_g[:, ic, 0:16],
                              xnh[:, ic, :], xnl[:, ic, :],
                              start=(ic == 0), stop=(ic == NCH_ - 1))
                c.done(ins, "mm")
                c.w(v, "mm")
                c.done(hilo(c, t1h[0:16, hsl], t1l[0:16, hsl], ps[0:16, 0, :]), "ev")
                c.w(t, "ev")
                if hf == 0:
                    c.w(y, "ev")
                    c.done(y.dma_start(out=wkg2s[:, :], in_=wkg2_h), "ld", 16)
                    c.done(y.dma_start(out=bkgcol[:, :], in_=bkg2_h), "ld", 16)
                    c.done(y.dma_start(out=bgcol[:, :], in_=bg_h), "ld", 16)
                    c.w(v, "ld")
                    c.done(hilo(c, wkg2h[0:16, :], wkg2l[0:16, :], wkg2s[:, :]), "ev")
                for oc in range(E_ // 256):
                    osl = slice(oc * 128, (oc + 1) * 128)
                    c.w(t, "ev")
                    ins = mm3(t, ps[:, 2, :], wkg2h[:, osl], wkg2l[:, osl],
                              t1h[:, hsl], t1l[:, hsl], start=True, stop=True)
                    c.done(ins, "mm")
                    c.w(v, "mm")
                    c.done(v.tensor_scalar(outc[:], ps[:, 2, :], bkgcol[:, oc:oc + 1],
                                           None, ALU.add), "ev")
                    c.w(s, "ev")
                    c.done(s.activation(sq[:], outc[:], AF.Abs), "sc")
                    c.w(s, "sc")
                    c.done(s.activation(sq[:], sq[:], AF.Exp, scale=-1.0), "sc")
                    c.w(s, "sc")
                    c.done(s.activation(sq[:], sq[:], AF.Ln, bias=1.0), "sc")
                    c.w(v, "sc")
                    c.done(v.tensor_single_scalar(xnc[:], outc[:], 0.0, ALU.min), "ev")
                    c.w(v, "ev")
                    c.done(v.tensor_tensor(outc[:], xnc[:], sq[:], ALU.subtract), "ev")
                    c.w(g, "ev")
                    c.done(g.dma_start(out=gk_o[oc * 128:(oc + 1) * 128, hsl],
                                       in_=outc[:]), "st", 16)
                    c.w(t, "st")
                    c.w(s, "st")
                for oc in range(NCH_):
                    stream_w_block(c, wv_h, oc, NCH_, wsp, wbh, wbl)
                    c.w(t, "ld", "ev")
                    for sub in range(NT // 128):
                        ssl = slice(sub * 128, (sub + 1) * 128)
                        for ic in range(NCH_):
                            ins = mm3(t, ps[:, 3, 0:128],
                                      xnh[:, ic, ssl], xnl[:, ic, ssl],
                                      wbh[:, ic, :], wbl[:, ic, :],
                                      start=(ic == 0), stop=(ic == NCH_ - 1))
                        c.done(ins, "mm")
                        c.w(v, "mm")
                        c.done(v.tensor_copy(outc[:, 0:128], ps[:, 3, 0:128]), "ev")
                        c.w(g, "ev")
                        tok0 = hf * NT + sub * 128
                        c.done(g.dma_start(out=v_o[tok0:tok0 + 128, oc * 128:(oc + 1) * 128],
                                           in_=outc[:, 0:128]), "st", 16)
                        c.w(t, "st")
                for oc in range(NCH_):
                    stream_w_block(c, wg_h, oc, NCH_, wsp, wbh, wbl)
                    c.w(t, "ld", "ev")
                    for ic in range(NCH_):
                        ins = mm3(t, ps[:, 0, :], wbh[:, ic, :], wbl[:, ic, :],
                                  xnh[:, ic, :], xnl[:, ic, :],
                                  start=(ic == 0), stop=(ic == NCH_ - 1))
                    c.done(ins, "mm")
                    c.w(v, "mm")
                    c.done(v.tensor_scalar(outc[:], ps[:, 0, :], bgcol[:, oc:oc + 1],
                                           None, ALU.add), "ev")
                    c.w(g, "ev")
                    c.done(g.dma_start(out=g_o[oc * 128:(oc + 1) * 128, hsl],
                                       in_=outc[:]), "st", 16)
                    c.w(t, "st")

            c.w(g, "st")
    return nc


def build_l3(S_=S, NHD=2):
    NCK = S_ // 128
    nc = bass.Bass("TRN2", target_bir_lowering=False, detect_race_conditions=False)
    c = Ctx(nc)
    t, v, s, g, y = c.t, c.v, c.s, c.g, c.y
    DV = 256

    q_h = nc.dram_tensor("q_g", [NHD, 128, S_], F32, kind="ExternalInput").ap()
    k_h = nc.dram_tensor("k_g", [NHD, 128, S_], F32, kind="ExternalInput").ap()
    gk_h = nc.dram_tensor("gk_g", [NHD, 128, S_], F32, kind="ExternalInput").ap()
    v_h = nc.dram_tensor("v_g", [NHD, S_, DV], F32, kind="ExternalInput").ap()
    o_h = nc.dram_tensor("o_g", [NHD, S_, DV], F32, kind="ExternalOutput").ap()

    qb = sbuf(nc, "qb", [128, S_], F32)
    kb = sbuf(nc, "kb", [128, S_], F32)
    gkb = sbuf(nc, "gkb", [128, S_], F32)
    vb = sbuf(nc, "vb", [128, S_ // 128, DV], F32)
    gkc = sbuf(nc, "gkc", [128, 128], F32)
    e1 = sbuf(nc, "e1", [128, 128], F32)
    qd = sbuf(nc, "qd", [128, 128], F32)
    knd = sbuf(nc, "knd", [128, 128], F32)
    kd = sbuf(nc, "kd", [128, 128], F32)
    kdt = sbuf(nc, "kdt", [128, 128], F32)
    amt = sbuf(nc, "amt", [128, 128], F32)
    Sb = sbuf(nc, "Sb", [128, DV], F32)
    ob = sbuf(nc, "ob", [128, DV], F32)
    osq = sbuf(nc, "osq", [128, DV], F32)
    triu = sbuf(nc, "triu", [128, 128], F32)
    idn = sbuf(nc, "idn", [128, 128], F32)
    onesb = sbuf(nc, "onesb", [128, 128], F32)
    dlc = sbuf(nc, "dlc", [128, 1], F32)
    mu_c = sbuf(nc, "mu_c", [128, 1], F32)
    s2_c = sbuf(nc, "s2_c", [128, 1], F32)
    rst_c = sbuf(nc, "rst_c", [128, 1], F32)
    nmu_c = sbuf(nc, "nmu_c", [128, 1], F32)
    zmat = sbuf(nc, "zmat", [128, 128], F32)

    psA = psum(nc, "psA", [128, 128])
    psO = psum(nc, "psO", [128, DV])
    psC = psum(nc, "psC", [128, DV])
    psT = psum(nc, "psT", [128, 128])

    with nc.Block() as block:
        @block.sync
        def _(sync):
            g.memset(onesb[:], 1.0)
            g.affine_select(triu[:], onesb[:], [[1, 128]], ALU.is_ge, 0.0,
                            base=0, channel_multiplier=-1)
            ins = g.affine_select(idn[:], onesb[:], [[-1, 128]], ALU.is_equal, 0.0,
                                  base=0, channel_multiplier=1)
            c.done(ins, "gst")
            c.w(v, "gst")
            c.done(v.memset(zmat[:], 0.0), "ev")

            for hd in range(NHD):
                c.w(y, "ev", "mm", "sc")
                c.done(y.dma_start(out=qb[:], in_=q_h[hd]), "ld", 16)
                c.done(y.dma_start(out=kb[:], in_=k_h[hd]), "ld", 16)
                c.done(y.dma_start(out=gkb[:], in_=gk_h[hd]), "ld", 16)
                c.done(y.dma_start(out=vb[:],
                                   in_=v_h[hd].rearrange("(ck p) d -> p ck d", p=128)), "ld", 16)
                c.w(v, "ld")
                c.done(v.memset(Sb[:], 0.0), "ev")
                for ck in range(NCK):
                    csl = slice(ck * 128, (ck + 1) * 128)
                    c.w(v, "sc", "mm")
                    c.done(v.tensor_tensor_scan(gkc[:], gkb[:, csl], zmat[:], 0.0, ALU.add, ALU.add), "ev")
                    c.w(s, "ev")
                    s.activation(e1[:], gkc[:], AF.Exp, scale=1.0 / 16.0)
                    c.done(s.activation(dlc[:], gkc[:, 127:128], AF.Exp,
                                        scale=1.0 / 16.0), "sc")
                    c.w(v, "sc")
                    c.done(v.tensor_tensor(qd[:], qb[:, csl], e1[:], ALU.mult), "ev")
                    c.w(s, "ev")
                    c.done(s.activation(e1[:], gkc[:], AF.Exp, scale=-1.0 / 16.0), "sc")
                    c.w(v, "sc")
                    c.done(v.tensor_tensor(knd[:], kb[:, csl], e1[:], ALU.mult), "ev")
                    c.w(v, "ev")
                    c.done(v.scalar_tensor_tensor(kd[:], knd[:], dlc[:], zmat[:], ALU.mult, ALU.add), "ev")
                    c.w(t, "ev")
                    c.done(t.matmul(psA[:], knd[:], qd[:], start=True, stop=True), "mm")
                    c.w(v, "mm")
                    c.done(v.tensor_tensor(amt[:], psA[:], triu[:], ALU.mult), "ev")
                    c.w(t, "ev")
                    t.matmul(psO[:], amt[:], vb[:, ck, :], start=True, stop=False)
                    c.done(t.matmul(psO[:], qd[:], Sb[:], start=False, stop=True), "mm")
                    t.wait_ge(c.sems["ev"], c.cnt["ev"])
                    c.done(t.transpose(psT[:], kd[:], idn[:]), "mm")
                    c.w(v, "mm")
                    c.done(v.tensor_copy(kdt[:], psT[:]), "ev")
                    c.w(t, "ev")
                    c.done(t.matmul(psC[:], kdt[:], vb[:, ck, :], start=True, stop=True), "mm")
                    c.w(v, "mm")
                    c.done(v.scalar_tensor_tensor(Sb[:], Sb[:], dlc[:], psC[:],
                                                  ALU.mult, ALU.add), "ev")
                    for fn in (
                        lambda: v.tensor_copy(ob[:], psO[:]),
                        lambda: v.reduce_sum(mu_c[:], ob[:], axis=AX.X),
                        lambda: v.tensor_scalar(mu_c[:], mu_c[:], 1.0 / DV, None, ALU.mult),
                        lambda: v.tensor_tensor(osq[:], ob[:], ob[:], ALU.mult),
                        lambda: v.reduce_sum(s2_c[:], osq[:], axis=AX.X),
                        lambda: v.tensor_scalar(s2_c[:], s2_c[:], 1.0 / DV, None, ALU.mult),
                        lambda: v.tensor_tensor(rst_c[:], mu_c[:], mu_c[:], ALU.mult),
                        lambda: v.tensor_tensor(s2_c[:], s2_c[:], rst_c[:], ALU.subtract),
                        lambda: v.tensor_scalar(s2_c[:], s2_c[:], 1.0, EPS_GLA, ALU.mult, ALU.add),
                        lambda: v.reciprocal(rst_c[:], s2_c[:]),
                    ):
                        c.done(fn(), "ev")
                        c.w(v, "ev")
                    c.w(s, "ev")
                    c.done(s.activation(rst_c[:], rst_c[:], AF.Sqrt), "sc")
                    c.w(v, "sc")
                    c.done(v.tensor_tensor(nmu_c[:], mu_c[:], rst_c[:], ALU.mult), "ev")
                    c.w(v, "ev")
                    c.done(v.tensor_scalar(nmu_c[:], nmu_c[:], -1.0, None, ALU.mult), "ev")
                    c.w(v, "ev")
                    c.done(v.tensor_scalar(ob[:], ob[:], rst_c[:], nmu_c[:],
                                           ALU.mult, ALU.add), "ev")
                    c.w(g, "ev")
                    c.done(g.dma_start(out=o_h[hd, csl, :], in_=ob[:]), "st", 16)
                    c.w(s, "st")
                    c.w(v, "st")
            c.w(g, "st")
    return nc


def _pos_index(W):
    return np.concatenate([np.arange(0, W, 2), np.arange(1, W, 2)])


def _expand_scales(scales, spans):
    s_full = np.repeat(scales.astype(np.float16).T, 128, axis=1)
    cols = []
    off = 0
    for W in spans:
        cols.append(s_full[:, off + _pos_index(W)])
        off += W
    return np.ascontiguousarray(np.concatenate(cols, axis=1))


def _bias_cols(b):
    return np.ascontiguousarray(b.reshape(-1, 128).T.astype(np.float32))


def kernel(L0_attn_a_x, L0_attn_a_w_qkv, L0_attn_a_w_out, L0_attn_a_b_out,
           L0_attn_b_Wq, L0_attn_b_Wk, L0_attn_b_Wkg1, L0_attn_b_Wkg2,
           L0_attn_b_bkg2, L0_attn_b_Wv, L0_attn_b_Wg, L0_attn_b_bg,
           L0_attn_b_Wo, L0_ffn_gate_up_packed, L0_ffn_gate_up_scales,
           L0_ffn_down_packed, L0_ffn_down_scales):
    f32 = np.float32
    x = np.asarray(L0_attn_a_x, dtype=f32)
    wqkv = np.asarray(L0_attn_a_w_qkv, dtype=f32)
    NTok = 1024
    cores = list(range(NCORE))

    xbT = [np.ascontiguousarray(x[b].T) for b in range(B)]

    nc1 = build_l1()
    in1 = []
    for c_ in cores:
        bc, qt = c_ // 4, c_ % 4
        h0 = 4 * qt
        in1.append({
            "xb": xbT[bc],
            "wq_t": np.ascontiguousarray(wqkv[h0 * 128:(h0 + 4) * 128, :].T),
            "wk_t": np.ascontiguousarray(wqkv[2048 + h0 * 128:2048 + (h0 + 4) * 128, :].T),
            "wv_t": np.ascontiguousarray(wqkv[4096 + h0 * 128:4096 + (h0 + 4) * 128, :].T),
        })
    r1 = run_bass_kernel_spmd(nc1, in1, core_ids=cores)
    attn_fm = []
    for b in range(B):
        cols = [np.asarray(r1.results[4 * b + qt]["aout"]) for qt in range(4)]
        attn_fm.append(np.ascontiguousarray(np.concatenate(cols, axis=1).T))

    DQW = 2048
    NSP = (I_F + DQW - 1) // DQW
    spans_dn = [min(DQW, I_F - i * DQW) for i in range(NSP)]
    gup_se = _expand_scales(np.asarray(L0_ffn_gate_up_scales), [E])
    dn_se = _expand_scales(np.asarray(L0_ffn_down_scales), spans_dn)
    gup_p = np.ascontiguousarray(np.asarray(L0_ffn_gate_up_packed))
    dn_p = np.ascontiguousarray(np.asarray(L0_ffn_down_packed))
    nc2 = build_l2(mode=2)
    in2 = []
    for c_ in cores:
        bc, qt = c_ // 4, c_ % 4
        tsl = slice(qt * NTok, (qt + 1) * NTok)
        in2.append({
            "aout_s": np.ascontiguousarray(attn_fm[bc][:, tsl]),
            "x_s": np.ascontiguousarray(xbT[bc][:, tsl]),
            "wout_t": np.ascontiguousarray(np.asarray(L0_attn_a_w_out, dtype=f32).T),
            "bout": _bias_cols(np.asarray(L0_attn_a_b_out, dtype=f32)),
            "gup_p": gup_p, "gup_se": gup_se, "dn_p": dn_p, "dn_se": dn_se,
            "wqg_t": np.ascontiguousarray(np.asarray(L0_attn_b_Wq, dtype=f32)),
            "wkg_t": np.ascontiguousarray(np.asarray(L0_attn_b_Wk, dtype=f32)),
            "wkg1_t": np.ascontiguousarray(np.asarray(L0_attn_b_Wkg1, dtype=f32)),
            "wkg2_t": np.ascontiguousarray(np.asarray(L0_attn_b_Wkg2, dtype=f32)),
            "bkg2": _bias_cols(np.asarray(L0_attn_b_bkg2, dtype=f32)),
            "wvg_t": np.ascontiguousarray(np.asarray(L0_attn_b_Wv, dtype=f32)),
            "wg_t": np.ascontiguousarray(np.asarray(L0_attn_b_Wg, dtype=f32)),
            "bg": _bias_cols(np.asarray(L0_attn_b_bg, dtype=f32)),
        })
    r2 = run_bass_kernel_spmd(nc2, in2, core_ids=cores)

    def gather(name, axis):
        out = []
        for b in range(B):
            parts = [np.asarray(r2.results[4 * b + qt][name]) for qt in range(4)]
            out.append(np.concatenate(parts, axis=axis))
        return out

    q_full = gather("q_o", 1)
    k_full = gather("k_o", 1)
    gk_full = gather("gk_o", 1)
    v_full = gather("v_o", 0)

    nc3 = build_l3()
    in3 = []
    for c_ in cores:
        bc, qt = c_ // 4, c_ % 4
        hds = (2 * qt, 2 * qt + 1)
        in3.append({
            "q_g": np.ascontiguousarray(np.stack(
                [q_full[bc][h * 128:(h + 1) * 128] for h in hds])),
            "k_g": np.ascontiguousarray(np.stack(
                [k_full[bc][h * 128:(h + 1) * 128] for h in hds])),
            "gk_g": np.ascontiguousarray(np.stack(
                [gk_full[bc][h * 128:(h + 1) * 128] for h in hds])),
            "v_g": np.ascontiguousarray(np.stack(
                [v_full[bc][:, h * 256:(h + 1) * 256] for h in hds])),
        })
    r3 = run_bass_kernel_spmd(nc3, in3, core_ids=cores)
    o_fm = []
    for b in range(B):
        cols = []
        for qt in range(4):
            og = np.asarray(r3.results[4 * b + qt]["o_g"])
            cols.extend([og[0], og[1]])
        o_fm.append(np.ascontiguousarray(np.concatenate(cols, axis=1).T))

    nc4 = build_l2(mode=4)
    in4 = []
    for c_ in cores:
        bc, qt = c_ // 4, c_ % 4
        tsl = slice(qt * NTok, (qt + 1) * NTok)
        in4.append({
            "o_s": np.ascontiguousarray(o_fm[bc][:, tsl]),
            "g_s": np.asarray(r2.results[c_]["g_o"]),
            "x_s": np.asarray(r2.results[c_]["x2_o"]),
            "wo_t": np.ascontiguousarray(np.asarray(L0_attn_b_Wo, dtype=f32)),
            "gup_p": gup_p, "gup_se": gup_se, "dn_p": dn_p, "dn_se": dn_se,
        })
    r4 = run_bass_kernel_spmd(nc4, in4, core_ids=cores)
    out = np.empty((B, S, E), dtype=f32)
    for b in range(B):
        parts = [np.asarray(r4.results[4 * b + qt]["xout_o"]) for qt in range(4)]
        out[b] = np.concatenate(parts, axis=1).T
    return out
